# version 02 (final)
# baseline (speedup 1.0000x reference)
import numpy as np
import ml_dtypes
from contextlib import ExitStack

import concourse.bass as bass
import concourse.mybir as mybir
from concourse.bass_utils import run_bass_kernel_spmd

F32 = mybir.dt.float32
BF16 = mybir.dt.bfloat16
I32 = mybir.dt.int32
AF = mybir.ActivationFunctionType
ALU = mybir.AluOpType

H = 2048
S = 2048
NH = 16
NFC = 16
NIC = 64
B_LOC = 2
GRP = 8
SCALE = float(1.0 / np.sqrt(128.0))
EPS = 1e-5

DEBUG_DUMPS = False


class Eng:
    SY = "sy"
    GD = "gd"
    PE = "pe"
    AC = "ac"
    VE = "ve"


ALL_ENGS = [Eng.SY, Eng.GD, Eng.PE, Eng.AC, Eng.VE]


class Prog:
    def __init__(self, sems):
        self.sems = sems
        self.ops = {e: [] for e in ALL_ENGS}

    def add(self, eng, fn, deps=(), tick=1):
        op = {}
        op["eng"], op["fn"], op["tick"] = eng, fn, tick
        op["deps"] = [d for d in deps if d is not None]
        sem = self.sems[eng]
        sem[1] += tick
        op["val"] = sem[1]
        op["idx"] = len(self.ops[eng])
        self.ops[eng].append(op)
        return (eng, op["val"], op["idx"])

    def emit(self, blk):
        sems = self.sems
        for eng in ALL_ENGS:
            ops = self.ops[eng]
            if not ops:
                continue

            def section(e, ops=ops, eng=eng):
                hw = {}
                for op in ops:
                    for (deng, dval, didx) in op["deps"]:
                        if deng == eng:
                            assert didx < op["idx"], "same-engine forward dep"
                            continue
                        if hw.get(deng, -1) < dval:
                            e.wait_ge(sems[deng][0], dval)
                            hw[deng] = dval
                    got = op["fn"](e)
                    if isinstance(got, list):
                        tot = 0
                        for inst, k in got:
                            inst.then_inc(sems[eng][0], k)
                            tot += k
                        assert tot == op["tick"], f"tick {tot} != {op['tick']}"
                    else:
                        got.then_inc(sems[eng][0], op["tick"])

            {Eng.SY: blk.sync, Eng.GD: blk.gpsimd, Eng.PE: blk.tensor,
             Eng.AC: blk.scalar, Eng.VE: blk.vector}[eng](section)


def build_program():
    nc = bass.Bass("TRN2", target_bir_lowering=False)

    x0 = nc.dram_tensor("x0", [B_LOC, H, S], F32, kind="ExternalInput")
    wq = nc.dram_tensor("wq", [8, 128, NFC * 256], F32, kind="ExternalInput")
    wk = nc.dram_tensor("wk", [8, 128, NFC * 256], F32, kind="ExternalInput")
    wv = nc.dram_tensor("wv", [8, 128, NFC * 256], F32, kind="ExternalInput")
    wo = nc.dram_tensor("wo", [8, 128, NFC * 256], F32, kind="ExternalInput")
    cosT = nc.dram_tensor("cosT", [1024, S], BF16, kind="ExternalInput")
    sinT = nc.dram_tensor("sinT", [1024, S], BF16, kind="ExternalInput")
    wgt = nc.dram_tensor("wgt", [NIC, 128, NFC * 128], I32, kind="ExternalInput")
    wut = nc.dram_tensor("wut", [NIC, 128, NFC * 128], I32, kind="ExternalInput")
    wdt = nc.dram_tensor("wdt", [NIC, 128, H], I32, kind="ExternalInput")
    gsc = nc.dram_tensor("gsc", [128, NIC], F32, kind="ExternalInput")
    usc = nc.dram_tensor("usc", [128, NIC], F32, kind="ExternalInput")
    dsc = nc.dram_tensor("dsc", [128, NFC], F32, kind="ExternalInput")
    gate = nc.dram_tensor("gate", [1, NH], F32, kind="ExternalInput")
    ident = nc.dram_tensor("ident", [128, 128], BF16, kind="ExternalInput")
    out = nc.dram_tensor("out", [B_LOC, H, S], F32, kind="ExternalOutput")
    comb_d = nc.dram_tensor("comb_d", [H, S], BF16, kind="Internal")
    xmid_d = nc.dram_tensor("xmid_d", [H, S], BF16, kind="Internal")

    dbg = {}
    if DEBUG_DUMPS:
        for nm, shp, dt in (("d_ln1", [128, NFC * S], BF16),
                            ("d_kr", [128, 2 * S], BF16),
                            ("d_qr", [128, 2 * S], BF16),
                            ("d_vtm", [128, 2 * S], BF16),
                            ("d_kfeat", [128, 2 * S], BF16),
                            ("d_comb", [H, S], BF16),
                            ("d_xmid", [H, S], BF16)):
            dbg[nm] = nc.dram_tensor(nm, shp, dt, kind="ExternalOutput")

    ctx = ExitStack()
    nc._kctx = ctx

    sems = {e: [ctx.enter_context(nc.semaphore("sem_" + e)), 0] for e in ALL_ENGS}

    def sbuf(name, shape, dtype, st=ctx):
        return st.enter_context(nc.sbuf_tensor(name, shape, dtype))

    ones_r = sbuf("ones_r", [1, 128], BF16)
    ones_c = sbuf("ones_c", [128, 1], BF16)
    s_id = sbuf("s_id", [128, 128], BF16)
    s_gate = sbuf("s_gate", [1, NH], F32)
    g_row = sbuf("g_row", [1, NH], F32)
    g1m_row = sbuf("g1m_row", [1, NH], F32)
    s_gsc = sbuf("s_gsc", [128, NIC], F32)
    s_usc = sbuf("s_usc", [128, NIC], F32)
    s_dsc = sbuf("s_dsc", [128, NFC], F32)
    rowf = sbuf("rowf", [1, 3 * S], F32)
    rows_a = sbuf("rows_a", [1, S], BF16)
    rows_b2 = sbuf("rows_b2", [1, S], BF16)
    s_eps = sbuf("s_eps", [1, 1], F32)

    out_dma_tokens = []

    def norm_rows(P, pst1, pst2, n_t, col0, dep_stats):
        rA = rowf[0:1, col0:col0 + n_t]
        rB = rowf[0:1, S + col0:S + col0 + n_t]
        rC = rowf[0:1, 2 * S + col0:2 * S + col0 + n_t]
        ra = rows_a[0:1, col0:col0 + n_t]
        rb = rows_b2[0:1, col0:col0 + n_t]

        def f_m(v):
            return [(v.tensor_scalar_mul(rA, pst1[0:1, 0:n_t], 1.0 / H), 1),
                    (v.tensor_scalar_mul(rB, pst2[0:1, 0:n_t], 1.0 / H), 1),
                    (v.tensor_mul(rC, rA, rA), 1),
                    (v.tensor_sub(rB, rB, rC), 1)]
        t_var = P.add(Eng.VE, f_m, deps=dep_stats, tick=4)

        def f_r(s):
            return [(s.activation(rC, rB, AF.Ln, bias=s_eps[0:1, 0:1]), 1),
                    (s.activation(rB, rC, AF.Exp, scale=-0.5), 1)]
        t_rstd = P.add(Eng.AC, f_r, deps=[t_var], tick=2)

        def f_b(v):
            return [(v.scalar_tensor_tensor(rC, rA, -1.0, rB, ALU.mult, ALU.mult), 1),
                    (v.tensor_copy(ra, rB), 1),
                    (v.tensor_copy(rb, rC), 1)]
        t_rows = P.add(Eng.VE, f_b, deps=[t_rstd], tick=3)
        return t_rows, t_var

    with nc.Block("prep", no_gpsimd_drain=True) as blk:
        P = Prog(sems)
        t_l1 = P.add(Eng.SY, lambda sy: [
            (sy.dma_start(out=s_id[:, :], in_=ident[:, :]), 16),
            (sy.dma_start(out=s_gate[:, :], in_=gate[:, :]), 16),
            (sy.dma_start(out=s_gsc[:, :], in_=gsc[:, :]), 16),
            (sy.dma_start(out=s_usc[:, :], in_=usc[:, :]), 16),
            (sy.dma_start(out=s_dsc[:, :], in_=dsc[:, :]), 16)], tick=80)
        P.add(Eng.VE, lambda v: [
            (v.memset(ones_r[:, :], 1.0), 1),
            (v.memset(ones_c[:, :], 1.0), 1),
            (v.memset(s_eps[:, :], EPS), 1)], tick=3)
        t_sig = P.add(Eng.AC, lambda s: s.activation(g_row[:, :], s_gate[:, :], AF.Sigmoid),
                      deps=[t_l1])
        P.add(Eng.VE, lambda v: v.tensor_scalar(g1m_row[:, :], g_row[:, :], -1.0, 1.0,
                                                ALU.mult, ALU.add), deps=[t_sig])
        P.emit(blk)

    for b in range(B_LOC):
        with ExitStack() as bctx:
            ln1 = sbuf(f"ln1_{b}", [128, NFC * S], BF16, bctx)
            wkb = sbuf(f"wkb_{b}", [128, NFC * 256], BF16, bctx)
            wqb = sbuf(f"wqb_{b}", [128, NFC * 256], BF16, bctx)
            wvb = sbuf(f"wvb_{b}", [128, NFC * 256], BF16, bctx)

            with ExitStack() as pc:
                xf_t = sbuf(f"p0xf_{b}", [128, NFC * 512], F32, pc)
                xb_t = sbuf(f"p0xb_{b}", [128, NFC * 512], BF16, pc)
                sq_t = sbuf(f"p0sq_{b}", [128, NFC * 512], BF16, pc)
                pst1 = pc.enter_context(nc.psum_tensor(f"p0st1_{b}", [128, 512], F32))
                pst2 = pc.enter_context(nc.psum_tensor(f"p0st2_{b}", [128, 512], F32))
                pba = pc.enter_context(nc.psum_tensor(f"p0ba_{b}", [128, 512], F32))
                pbb = pc.enter_context(nc.psum_tensor(f"p0bb_{b}", [128, 512], F32))
                with nc.Block(f"p0_{b}", no_gpsimd_drain=True) as blk:
                    P = Prog(sems)
                    toks = []
                    for wsrc, wdst in ((wk, wkb), (wq, wqb), (wv, wvb)):
                        toks.append(P.add(Eng.GD, lambda g, wsrc=wsrc, wdst=wdst:
                                          g.dma_start(out=wdst[:, :], in_=wsrc[0, :, :]),
                                          tick=16))
                    pair_w_loaded = toks
                    cs_toks = []

                    prev_apply = None
                    prev_rows = None
                    prev_s1 = None
                    prev_s2 = None
                    prev_sq = None
                    for j in range(4):
                        t0 = j * 512

                        def f_xf(sy, t0=t0):
                            return [(sy.dma_start(out=xf_t[:, fc * 512:(fc + 1) * 512],
                                                  in_=x0[b, fc * 128:(fc + 1) * 128, t0:t0 + 512]), 16)
                                    for fc in range(NFC)]
                        t_xf = P.add(Eng.SY, f_xf, deps=[prev_apply], tick=256)

                        def f_xb(g, t0=t0):
                            return [(g.dma_start(out=xb_t[:, fc * 512:(fc + 1) * 512],
                                                 in_=x0[b, fc * 128:(fc + 1) * 128, t0:t0 + 512]), 16)
                                    for fc in range(NFC)]
                        t_xb = P.add(Eng.GD, f_xb, deps=[prev_s1, prev_sq], tick=256)

                        def f_sq(v):
                            return [(v.tensor_mul(sq_t[:, fc * 512:(fc + 1) * 512],
                                                  xb_t[:, fc * 512:(fc + 1) * 512],
                                                  xb_t[:, fc * 512:(fc + 1) * 512]), 1)
                                    for fc in range(NFC)]
                        t_sq = P.add(Eng.VE, f_sq, deps=[t_xb, prev_s2], tick=NFC)

                        def f_s1(t):
                            for fc in range(NFC):
                                mi = t.matmul(pst1[0:1, :], ones_c[:, :],
                                              xb_t[:, fc * 512:(fc + 1) * 512],
                                              start=(fc == 0), stop=(fc == NFC - 1))
                            return mi
                        t_s1 = P.add(Eng.PE, f_s1, deps=[t_xb, prev_rows])

                        def f_s2(t):
                            for fc in range(NFC):
                                mi = t.matmul(pst2[0:1, :], ones_c[:, :],
                                              sq_t[:, fc * 512:(fc + 1) * 512],
                                              start=(fc == 0), stop=(fc == NFC - 1))
                            return mi
                        t_s2 = P.add(Eng.PE, f_s2, deps=[t_sq, prev_rows])

                        t_rows, t_rd = norm_rows(P, pst1, pst2, 512, t0, [t_s1, t_s2])
                        prev_rows = t_rd
                        prev_s1, prev_s2, prev_sq = t_s1, t_s2, t_sq

                        def f_bc(t, t0=t0):
                            return [(t.matmul(pba[:, :], ones_r[:, :],
                                              rows_a[0:1, t0:t0 + 512], start=True, stop=True), 1),
                                    (t.matmul(pbb[:, :], ones_r[:, :],
                                              rows_b2[0:1, t0:t0 + 512], start=True, stop=True), 1)]
                        t_bc = P.add(Eng.PE, f_bc, deps=[t_rows, prev_apply], tick=2)

                        def f_apply(v, t0=t0):
                            insts = []
                            for fc in range(NFC):
                                xfc = xf_t[:, fc * 512:(fc + 1) * 512]
                                insts.append((v.tensor_mul(xfc, xfc, pba[:, :]), 1))
                                insts.append((v.tensor_add(
                                    ln1[:, fc * S + t0: fc * S + t0 + 512], xfc, pbb[:, :]), 1))
                            return insts
                        prev_apply = P.add(Eng.VE, f_apply, deps=[t_bc, t_xf], tick=2 * NFC)
                    if DEBUG_DUMPS and b == 0:
                        P.add(Eng.SY, lambda sy: sy.dma_start(out=dbg["d_ln1"][:, :], in_=ln1[:, :]),
                              deps=[prev_apply], tick=16)
                    P.emit(blk)

            with ExitStack() as pairctx:
                cs_c = sbuf(f"cs_c_{b}", [128, S], BF16, pairctx)
                cs_s = sbuf(f"cs_s_{b}", [128, S], BF16, pairctx)
                kpre = sbuf(f"kpre_{b}", [128, 2 * S], BF16, pairctx)
                qpre = sbuf(f"qpre_{b}", [128, 2 * S], BF16, pairctx)
                kr = sbuf(f"kr_{b}", [128, 2 * S], BF16, pairctx)
                qr = sbuf(f"qr_{b}", [128, 2 * S], BF16, pairctx)
                tmpA = sbuf(f"tmpA_{b}", [128, S], BF16, pairctx)
                tmpB = sbuf(f"tmpB_{b}", [128, S], BF16, pairctx)
                vfm = sbuf(f"vfm_{b}", [128, S], BF16, pairctx)
                vtm = sbuf(f"vtm_{b}", [128, 2 * S], BF16, pairctx)
                kftm = sbuf(f"kftm_{b}", [128, S], BF16, pairctx)
                mem_b = sbuf(f"memb_{b}", [128, 256], BF16, pairctx)
                nt_f = sbuf(f"ntf_{b}", [128, 2], F32, pairctx)
                nt_b = sbuf(f"ntb_{b}", [128, 2], BF16, pairctx)
                s_S = sbuf(f"sS_{b}", [128, S], F32, pairctx)
                s_PT = sbuf(f"sPT_{b}", [128, 2 * S], BF16, pairctx)
                m_un = vfm
                comb = kftm
                ctmp1 = sbuf(f"ctmp1_{b}", [128, 1024], BF16, pairctx)
                ctmp2 = sbuf(f"ctmp2_{b}", [128, 1024], BF16, pairctx)
                kfeat = kpre
                qfeat = qpre
                s_Sb = tmpA
                a_un = tmpB

                for p in range(8):
                    pair_deps = pair_w_loaded
                    with ExitStack() as pc:
                        p_pre = pc.enter_context(nc.psum_tensor(f"ppre_{b}_{p}", [128, S], F32))
                        p_tr = pc.enter_context(nc.psum_tensor(f"ptr_{b}_{p}", [128, 1024], BF16))
                        p_mem = pc.enter_context(nc.psum_tensor(f"pmem_{b}_{p}", [128, 128], F32))
                        with nc.Block(f"proj_{b}_{p}", no_gpsimd_drain=True) as blk:
                            P = Prog(sems)
                            if p == 0:
                                cs_toks = [
                                    P.add(Eng.GD, lambda g: g.dma_start(
                                        out=cs_c[:, :], in_=cosT[0:128, :]), tick=16),
                                    P.add(Eng.GD, lambda g: g.dma_start(
                                        out=cs_s[:, :], in_=sinT[0:128, :]), tick=16)]

                            def mm_ns(t, wbuf, hh, ns):
                                for fc in range(NFC):
                                    mi = t.matmul(
                                        p_pre[:, ns * 512:(ns + 1) * 512],
                                        wbuf[:, fc * 256 + hh * 128: fc * 256 + hh * 128 + 128],
                                        ln1[:, fc * S + ns * 512: fc * S + (ns + 1) * 512],
                                        start=(fc == 0), stop=(fc == NFC - 1))
                                return mi

                            prev_ev_ns = [None] * 4
                            evacs = []
                            kq_mms = []
                            for wi, (wbuf, dstbuf) in enumerate(((wkb, kpre), (wkb, kpre),
                                                                 (wqb, qpre), (wqb, qpre))):
                                hh = wi % 2
                                last_ev = None
                                last_mm = None
                                for ns in range(4):
                                    t_mm = P.add(Eng.PE,
                                                 lambda t, wbuf=wbuf, hh=hh, ns=ns:
                                                 mm_ns(t, wbuf, hh, ns),
                                                 deps=list(pair_deps) + [prev_ev_ns[ns]])
                                    t_ev = P.add(Eng.AC,
                                                 lambda s, dstbuf=dstbuf, hh=hh, ns=ns:
                                                 s.activation(
                                                     dstbuf[:, hh * S + ns * 512: hh * S + (ns + 1) * 512],
                                                     p_pre[:, ns * 512:(ns + 1) * 512], AF.Copy),
                                                 deps=[t_mm])
                                    prev_ev_ns[ns] = t_ev
                                    last_ev = t_ev
                                    last_mm = t_mm
                                evacs.append(last_ev)
                                kq_mms.append(last_mm)

                            def rope(v, src, dst):
                                return [(v.tensor_mul(tmpA[:, :], src[:, 0:S], cs_c[:, :]), 1),
                                        (v.tensor_mul(tmpB[:, :], src[:, S:2 * S], cs_s[:, :]), 1),
                                        (v.tensor_sub(dst[:, 0:S], tmpA[:, :], tmpB[:, :]), 1),
                                        (v.tensor_mul(tmpA[:, :], src[:, S:2 * S], cs_c[:, :]), 1),
                                        (v.tensor_mul(tmpB[:, :], src[:, 0:S], cs_s[:, :]), 1),
                                        (v.tensor_add(dst[:, S:2 * S], tmpA[:, :], tmpB[:, :]), 1)]
                            t_ropek = P.add(Eng.VE, lambda v: rope(v, kpre, kr),
                                            deps=[evacs[0], evacs[1]] + list(cs_toks), tick=6)
                            t_ropeq = P.add(Eng.VE, lambda v: rope(v, qpre, qr),
                                            deps=[evacs[2], evacs[3]] + list(cs_toks), tick=6)

                            vtm_ev = []
                            prev_tr_ev = None
                            v_mm_last = None
                            for hh in range(2):
                                last_ev = None
                                for ns in range(4):
                                    t_mm = P.add(Eng.PE,
                                                 lambda t, hh=hh, ns=ns: mm_ns(t, wvb, hh, ns),
                                                 deps=[prev_ev_ns[ns]])
                                    dep_e = [t_mm] + ([t_tr] if hh == 1 else [])
                                    t_ev = P.add(Eng.AC,
                                                 lambda s, ns=ns: s.activation(
                                                     vfm[:, ns * 512:(ns + 1) * 512],
                                                     p_pre[:, ns * 512:(ns + 1) * 512], AF.Copy),
                                                 deps=dep_e)
                                    prev_ev_ns[ns] = t_ev
                                    last_ev = t_ev
                                v_mm_last = t_mm
                                t_vev = last_ev
                                for tb in range(2):
                                    def f_tr(t, tb=tb):
                                        for tc in range(8):
                                            tt = tb * 8 + tc
                                            mi = t.transpose(p_tr[:, tc * 128:(tc + 1) * 128],
                                                             vfm[:, tt * 128:(tt + 1) * 128],
                                                             s_id[:, :])
                                        return mi
                                    t_tr = P.add(Eng.PE, f_tr, deps=[t_vev, prev_tr_ev])

                                    def f_trev(v, hh=hh, tb=tb):
                                        return v.tensor_copy(
                                            vtm[:, hh * S + tb * 1024: hh * S + (tb + 1) * 1024],
                                            p_tr[:, :])
                                    prev_tr_ev = P.add(Eng.VE, f_trev, deps=[t_tr])
                                    vtm_ev.append(prev_tr_ev)

                            feat_toks = []
                            for fi, (srcbuf, dstbuf, hh) in enumerate(
                                    ((kr, kfeat, 0), (kr, kfeat, 1), (qr, qfeat, 0), (qr, qfeat, 1))):
                                dep0 = t_ropek if srcbuf is kr else t_ropeq
                                t_min = P.add(Eng.VE,
                                              lambda v, srcbuf=srcbuf, hh=hh:
                                              v.tensor_scalar_min(
                                                  tmpA[:, :], srcbuf[:, hh * S:(hh + 1) * S], 0.0),
                                              deps=[dep0] + ([t_f] if feat_toks else [t_ropeq]))
                                t_exp = P.add(Eng.AC,
                                              lambda s: s.activation(tmpB[:, :], tmpA[:, :], AF.Exp),
                                              deps=[t_min])
                                kw = (dstbuf is kfeat)

                                def f_feat(v, srcbuf=srcbuf, dstbuf=dstbuf, hh=hh, kw=kw):
                                    if kw:
                                        return v.scalar_tensor_tensor(
                                            dstbuf[:, hh * S:(hh + 1) * S],
                                            srcbuf[:, hh * S:(hh + 1) * S], 0.0, tmpB[:, :],
                                            ALU.max, ALU.add, accum_out=nt_f[:, hh:hh + 1])
                                    return v.scalar_tensor_tensor(
                                        dstbuf[:, hh * S:(hh + 1) * S],
                                        srcbuf[:, hh * S:(hh + 1) * S], 0.0, tmpB[:, :],
                                        ALU.max, ALU.add)
                                t_f = P.add(Eng.VE, f_feat, deps=[t_exp])
                                feat_toks.append(t_f)

                            t_ntb = P.add(Eng.VE, lambda v: v.tensor_copy(nt_b[:, :], nt_f[:, :]),
                                          deps=[feat_toks[1]])

                            mem_ev = []
                            mem_mm = []
                            for hh in range(2):
                                kf_ev = []
                                for tb in range(2):
                                    def f_tr(t, hh=hh, tb=tb):
                                        for tc in range(8):
                                            tt = tb * 8 + tc
                                            mi = t.transpose(
                                                p_tr[:, tc * 128:(tc + 1) * 128],
                                                kfeat[:, hh * S + tt * 128: hh * S + (tt + 1) * 128],
                                                s_id[:, :])
                                        return mi
                                    t_tr = P.add(Eng.PE, f_tr, deps=[feat_toks[hh], prev_tr_ev])

                                    def f_trev(v, tb=tb):
                                        return v.tensor_copy(kftm[:, tb * 1024:(tb + 1) * 1024],
                                                             p_tr[:, :])
                                    prev_tr_ev = P.add(Eng.VE, f_trev, deps=[t_tr])
                                    kf_ev.append(prev_tr_ev)

                                def f_memmm(t, hh=hh):
                                    for tc in range(NFC):
                                        mi = t.matmul(
                                            p_mem[:, :],
                                            kftm[:, tc * 128:(tc + 1) * 128],
                                            vtm[:, hh * S + tc * 128: hh * S + (tc + 1) * 128],
                                            start=(tc == 0), stop=(tc == NFC - 1))
                                    return mi
                                t_memmm = P.add(Eng.PE, f_memmm,
                                                deps=kf_ev + [vtm_ev[2 * hh + 1]] +
                                                ([mem_ev[-1]] if mem_ev else []))
                                mem_mm.append(t_memmm)
                                t_mev = P.add(Eng.AC,
                                              lambda s, hh=hh: s.activation(
                                                  mem_b[:, hh * 128:(hh + 1) * 128],
                                                  p_mem[:, :], AF.Copy),
                                              deps=[t_memmm])
                                mem_ev.append(t_mev)

                            pair_w_cons = {"w": [kq_mms[-1], v_mm_last],
                                           "cs": [t_ropek, t_ropeq]}
                            proj_final = {"mem": mem_ev, "ntb": t_ntb,
                                          "feats": feat_toks, "vtm": vtm_ev,
                                          "ropek": t_ropek, "ropeq": t_ropeq,
                                          "kfeat_tr_done": prev_tr_ev}
                            if DEBUG_DUMPS and b == 0 and p == 0:
                                P.add(Eng.SY, lambda sy: sy.dma_start(out=dbg["d_kr"][:, :], in_=kr[:, :]),
                                      deps=[t_ropek], tick=16)
                                P.add(Eng.SY, lambda sy: sy.dma_start(out=dbg["d_qr"][:, :], in_=qr[:, :]),
                                      deps=[t_ropeq], tick=16)
                                P.add(Eng.SY, lambda sy: sy.dma_start(out=dbg["d_vtm"][:, :], in_=vtm[:, :]),
                                      deps=vtm_ev, tick=16)
                                P.add(Eng.SY, lambda sy: sy.dma_start(out=dbg["d_kfeat"][:, :], in_=kfeat[:, :]),
                                      deps=feat_toks, tick=16)
                            P.emit(blk)

                    with ExitStack() as pc:
                        p_scA = pc.enter_context(nc.psum_tensor(f"pscA_{b}_{p}", [128, 1024], F32))
                        p_scB = pc.enter_context(nc.psum_tensor(f"pscB_{b}_{p}", [128, 1024], F32))
                        p_att = pc.enter_context(nc.psum_tensor(f"patt_{b}_{p}", [128, S], F32))
                        p_sc = [p_scA, p_scB]
                        with nc.Block(f"attn_{b}_{p}", no_gpsimd_drain=True) as blk:
                            P = Prog(sems)

                            if p < 7:
                                np_pair = p + 1
                                toks = []
                                for wsrc, wdst in ((wk, wkb), (wq, wqb), (wv, wvb)):
                                    toks.append(P.add(
                                        Eng.GD,
                                        lambda g, wsrc=wsrc, wdst=wdst, np_pair=np_pair:
                                        g.dma_start(out=wdst[:, :], in_=wsrc[np_pair, :, :]),
                                        deps=pair_w_cons["w"], tick=16))
                                pair_w_loaded = toks
                                cs_toks = [
                                    P.add(Eng.GD, lambda g, np_pair=np_pair:
                                          g.dma_start(out=cs_c[:, :],
                                                      in_=cosT[np_pair * 128:(np_pair + 1) * 128, :]),
                                          deps=pair_w_cons["cs"], tick=16),
                                    P.add(Eng.GD, lambda g, np_pair=np_pair:
                                          g.dma_start(out=cs_s[:, :],
                                                      in_=sinT[np_pair * 128:(np_pair + 1) * 128, :]),
                                          deps=pair_w_cons["cs"], tick=16)]

                            prev_head_cons = []
                            prev_at = {}
                            prev_Sa = {}
                            for hh in range(2):
                                h_abs = p + 8 * hh
                                T_exp = {}
                                T_sc = {}
                                at_last = {}
                                S_last = {}
                                for kc in range(16):
                                    par = kc % 2
                                    for qh in range(2):
                                        def f_sc(t, hh=hh, kc=kc, qh=qh):
                                            for ns in range(2):
                                                q0 = qh * 1024 + ns * 512
                                                mi = t.matmul(
                                                    p_sc[qh][:, ns * 512:(ns + 1) * 512],
                                                    kr[:, hh * S + kc * 128: hh * S + (kc + 1) * 128],
                                                    qr[:, hh * S + q0: hh * S + q0 + 512],
                                                    start=True, stop=True)
                                            return mi
                                        deps = [T_exp.get((kc - 1, qh))]
                                        if kc == 0:
                                            deps += prev_head_cons
                                            if hh == 0:
                                                deps += [proj_final["ropek"], proj_final["ropeq"]]
                                        T_sc[(kc, qh)] = P.add(Eng.PE, f_sc, deps=deps)
                                    for qh in range(2):
                                        def f_exp(s, hh=hh, kc=kc, qh=qh, par=par):
                                            return s.activation(
                                                s_PT[:, par * S + qh * 1024: par * S + (qh + 1) * 1024],
                                                p_sc[qh][:, :], AF.Exp, scale=SCALE)
                                        T_exp[(kc, qh)] = P.add(
                                            Eng.AC, f_exp,
                                            deps=[T_sc[(kc, qh)], prev_at.get((kc - 2, qh)),
                                                  prev_Sa.get((kc - 2, qh))])
                                    for qh in range(2):
                                        def f_at(t, hh=hh, kc=kc, qh=qh, par=par):
                                            for ns in range(2):
                                                q0 = qh * 1024 + ns * 512
                                                mi = t.matmul(
                                                    p_att[:, q0:q0 + 512],
                                                    vtm[:, hh * S + kc * 128: hh * S + (kc + 1) * 128],
                                                    s_PT[:, par * S + q0: par * S + q0 + 512],
                                                    start=(kc == 0), stop=(kc == 15))
                                            return mi
                                        at_deps = [T_exp[(kc, qh)]]
                                        if kc == 0:
                                            at_deps += prev_head_cons + [proj_final["vtm"][2 * hh + 1]]
                                        t_at = P.add(Eng.PE, f_at, deps=at_deps)
                                        prev_at[(kc, qh)] = t_at
                                        at_last[qh] = t_at

                                        def f_S(v, kc=kc, qh=qh, par=par):
                                            src = s_PT[:, par * S + qh * 1024: par * S + (qh + 1) * 1024]
                                            dstS = s_S[:, qh * 1024:(qh + 1) * 1024]
                                            if kc == 0:
                                                return v.tensor_copy(dstS, src)
                                            return v.tensor_add(dstS, dstS, src)
                                        S_deps = [T_exp[(kc, qh)]]
                                        if kc == 0:
                                            S_deps += prev_head_cons
                                        eng_S = Eng.VE if qh == 0 else Eng.GD
                                        t_Sa = P.add(eng_S, f_S, deps=S_deps)
                                        prev_Sa[(kc, qh)] = t_Sa
                                        S_last[qh] = t_Sa

                                t_evA = P.add(Eng.AC,
                                              lambda s: s.activation(a_un[:, :], p_att[:, :], AF.Copy),
                                              deps=[at_last[0], at_last[1]] + prev_head_cons)

                                def f_memT(t, hh=hh):
                                    for ns in range(4):
                                        mi = t.matmul(
                                            p_att[:, ns * 512:(ns + 1) * 512],
                                            mem_b[:, hh * 128:(hh + 1) * 128],
                                            qfeat[:, hh * S + ns * 512: hh * S + (ns + 1) * 512],
                                            start=True, stop=True)
                                    return mi
                                t_memT = P.add(Eng.PE, f_memT,
                                               deps=[t_evA, proj_final["mem"][hh],
                                                     proj_final["feats"][2 + hh]])

                                t_Sb = P.add(Eng.VE,
                                             lambda v: v.tensor_copy(s_Sb[:, :], s_S[:, :]),
                                             deps=[S_last[0], S_last[1]])

                                def f_dens(t):
                                    insts = []
                                    for ns in range(4):
                                        pt = p_sc[ns // 2]
                                        c0 = (ns % 2) * 512
                                        insts.append((t.matmul(pt[0:1, c0:c0 + 512], ones_c[:, :],
                                                               s_Sb[:, ns * 512:(ns + 1) * 512],
                                                               start=True, stop=True), 1))
                                    return insts
                                t_dens = P.add(Eng.PE, f_dens,
                                               deps=[t_Sb, T_exp[(15, 0)], T_exp[(15, 1)]], tick=4)

                                def f_dens_ev(v):
                                    insts = []
                                    for ns in range(4):
                                        pt = p_sc[ns // 2]
                                        c0 = (ns % 2) * 512
                                        insts.append((v.tensor_copy(
                                            rowf[0:1, ns * 512:(ns + 1) * 512],
                                            pt[0:1, c0:c0 + 512]), 1))
                                    return insts
                                t_dsev = P.add(Eng.VE, f_dens_ev, deps=[t_dens], tick=4)

                                def f_denl(t, hh=hh):
                                    insts = []
                                    for ns in range(4):
                                        pt = p_sc[ns // 2]
                                        c0 = (ns % 2) * 512
                                        insts.append((t.matmul(
                                            pt[0:1, c0:c0 + 512], nt_b[:, hh:hh + 1],
                                            qfeat[:, hh * S + ns * 512: hh * S + (ns + 1) * 512],
                                            start=True, stop=True), 1))
                                    return insts
                                t_denl = P.add(Eng.PE, f_denl,
                                               deps=[t_dsev, proj_final["ntb"]], tick=4)

                                def f_denl_ev(v):
                                    insts = []
                                    for ns in range(4):
                                        pt = p_sc[ns // 2]
                                        c0 = (ns % 2) * 512
                                        insts.append((v.tensor_copy(
                                            rowf[0:1, S + ns * 512:S + (ns + 1) * 512],
                                            pt[0:1, c0:c0 + 512]), 1))
                                    return insts
                                t_dlev = P.add(Eng.VE, f_denl_ev, deps=[t_denl], tick=4)

                                t_evM = P.add(Eng.AC,
                                              lambda s: s.activation(m_un[:, :], p_att[:, :], AF.Copy),
                                              deps=[t_memT])

                                def f_rr(v, h_abs=h_abs):
                                    rC = rowf[0:1, 2 * S:3 * S]
                                    return [
                                        (v.reciprocal(rC, rowf[0:1, 0:S]), 1),
                                        (v.tensor_single_scalar(rC, rC,
                                                                g1m_row[0:1, h_abs:h_abs + 1],
                                                                ALU.mult), 1),
                                        (v.tensor_copy(rows_a[0:1, :], rC), 1),
                                        (v.reciprocal(rC, rowf[0:1, S:2 * S]), 1),
                                        (v.tensor_single_scalar(rC, rC,
                                                                g_row[0:1, h_abs:h_abs + 1],
                                                                ALU.mult), 1),
                                        (v.tensor_copy(rows_b2[0:1, :], rC), 1),
                                    ]
                                t_rr = P.add(Eng.VE, f_rr, deps=[t_dlev], tick=6)

                                def f_B1(t):
                                    insts = []
                                    for ns in range(4):
                                        pt = p_sc[ns // 2]
                                        c0 = (ns % 2) * 512
                                        insts.append((t.matmul(pt[:, c0:c0 + 512], ones_r[:, :],
                                                               rows_b2[0:1, ns * 512:(ns + 1) * 512],
                                                               start=True, stop=True), 1))
                                    return insts
                                t_B1 = P.add(Eng.PE, f_B1, deps=[t_rr, t_dlev], tick=4)

                                def f_B2(t):
                                    insts = []
                                    for ns in range(4):
                                        insts.append((t.matmul(
                                            p_att[:, ns * 512:(ns + 1) * 512], ones_r[:, :],
                                            rows_a[0:1, ns * 512:(ns + 1) * 512],
                                            start=True, stop=True), 1))
                                    return insts
                                t_B2 = P.add(Eng.PE, f_B2, deps=[t_rr, t_evM], tick=4)

                                comb_toks = []
                                for qh in range(2):
                                    def f_comb(v, qh=qh):
                                        c0 = qh * 1024
                                        return [
                                            (v.tensor_mul(ctmp1[:, :], m_un[:, c0:c0 + 1024],
                                                          p_sc[qh][:, :]), 1),
                                            (v.tensor_mul(ctmp2[:, :], a_un[:, c0:c0 + 1024],
                                                          p_att[:, c0:c0 + 1024]), 1),
                                            (v.tensor_add(comb[:, c0:c0 + 1024],
                                                          ctmp1[:, :], ctmp2[:, :]), 1),
                                        ]
                                    comb_toks.append(P.add(Eng.VE, f_comb,
                                                           deps=[t_B1, t_B2, t_evA, t_evM], tick=3))

                                t_cdma = P.add(Eng.SY,
                                               lambda sy, h_abs=h_abs: sy.dma_start(
                                                   out=comb_d[h_abs * 128:(h_abs + 1) * 128, :],
                                                   in_=comb[:, :]),
                                               deps=comb_toks, tick=16)

                                prev_head_cons = comb_toks + [t_cdma, t_evA, t_evM]
                                prev_at = {(kc - 16, qh): v for (kc, qh), v in prev_at.items()}
                                prev_Sa = {(kc - 16, qh): v for (kc, qh), v in prev_Sa.items()}

                            P.emit(blk)

            with ExitStack() as pc:
                op_a = sbuf(f"opa_{b}", [128, NFC * S], BF16, pc)
                wob = [wkb, wqb]
                sq_t = sbuf(f"wosq_{b}", [128, NFC * 512], BF16, pc)
                xfc_bufs = [sbuf(f"woxf{i}_{b}", [128, 512], F32, pc) for i in range(2)]
                xm_bufs = [sbuf(f"woxm{i}_{b}", [128, 512], BF16, pc) for i in range(2)]
                p_out = pc.enter_context(nc.psum_tensor(f"pwout_{b}", [128, S], F32))
                pst1 = pc.enter_context(nc.psum_tensor(f"pwst1_{b}", [128, 512], F32))
                pst2 = pc.enter_context(nc.psum_tensor(f"pwst2_{b}", [128, 512], F32))
                pba = pc.enter_context(nc.psum_tensor(f"pwba_{b}", [128, 512], F32))
                pbb = pc.enter_context(nc.psum_tensor(f"pwbb_{b}", [128, 512], F32))
                comb_a = ln1
                with nc.Block(f"wo_{b}", no_gpsimd_drain=True) as blk:
                    P = Prog(sems)

                    def f_crel(sy):
                        return [(sy.dma_start(out=comb_a[:, hc * S:(hc + 1) * S],
                                              in_=comb_d[hc * 128:(hc + 1) * 128, :]), 16)
                                for hc in range(NFC)]
                    t_crel = P.add(Eng.SY, f_crel, tick=256)

                    last_mm_buf = {0: None, 1: None}
                    ev_all = []
                    wo_prev_ev = [None] * 4
                    for ob in range(8):
                        wbuf = wob[ob % 2]
                        t_wol = P.add(Eng.GD,
                                      lambda g, ob=ob, wbuf=wbuf:
                                      g.dma_start(out=wbuf[:, :], in_=wo[ob, :, :]),
                                      deps=[last_mm_buf[ob % 2]], tick=16)
                        for oi in range(2):
                            oc = ob * 2 + oi

                            for ns in range(4):
                                def f_mm(t, wbuf=wbuf, oi=oi, ns=ns):
                                    for hc in range(NFC):
                                        mi = t.matmul(
                                            p_out[:, ns * 512:(ns + 1) * 512],
                                            wbuf[:, hc * 256 + oi * 128: hc * 256 + oi * 128 + 128],
                                            comb_a[:, hc * S + ns * 512: hc * S + (ns + 1) * 512],
                                            start=(hc == 0), stop=(hc == NFC - 1))
                                    return mi
                                t_mm = P.add(Eng.PE, f_mm,
                                             deps=[t_wol, t_crel, wo_prev_ev[ns]])
                                t_ev = P.add(Eng.AC,
                                             lambda s, oc=oc, ns=ns: s.activation(
                                                 op_a[:, oc * S + ns * 512: oc * S + (ns + 1) * 512],
                                                 p_out[:, ns * 512:(ns + 1) * 512], AF.Copy),
                                             deps=[t_mm])
                                wo_prev_ev[ns] = t_ev
                            last_mm_buf[ob % 2] = t_mm
                            ev_all.append(t_ev)

                    prev_rows = None
                    prev_s2 = None
                    prev_fins = []
                    for j in range(4):
                        t0 = j * 512

                        def f_sq(v, t0=t0):
                            insts = []
                            for oc in range(NFC):
                                src = op_a[:, oc * S + t0: oc * S + t0 + 512]
                                insts.append((v.tensor_mul(sq_t[:, oc * 512:(oc + 1) * 512],
                                                           src, src), 1))
                            return insts
                        t_sq = P.add(Eng.VE, f_sq, deps=ev_all + [prev_s2], tick=NFC)

                        def f_s1(t, t0=t0):
                            for oc in range(NFC):
                                mi = t.matmul(pst1[0:1, :], ones_c[:, :],
                                              op_a[:, oc * S + t0: oc * S + t0 + 512],
                                              start=(oc == 0), stop=(oc == NFC - 1))
                            return mi
                        t_s1 = P.add(Eng.PE, f_s1, deps=ev_all + [prev_rows])

                        def f_s2(t):
                            for oc in range(NFC):
                                mi = t.matmul(pst2[0:1, :], ones_c[:, :],
                                              sq_t[:, oc * 512:(oc + 1) * 512],
                                              start=(oc == 0), stop=(oc == NFC - 1))
                            return mi
                        t_s2 = P.add(Eng.PE, f_s2, deps=[t_sq, prev_rows])
                        prev_s2 = t_s2

                        t_rows, t_rd = norm_rows(P, pst1, pst2, 512, t0, [t_s1, t_s2])
                        prev_rows = t_rd

                        def f_bc(t, t0=t0):
                            return [(t.matmul(pba[:, :], ones_r[:, :], rows_a[0:1, t0:t0 + 512],
                                              start=True, stop=True), 1),
                                    (t.matmul(pbb[:, :], ones_r[:, :], rows_b2[0:1, t0:t0 + 512],
                                              start=True, stop=True), 1)]
                        t_bc = P.add(Eng.PE, f_bc, deps=[t_rows] + prev_fins, tick=2)

                        fins = []
                        for fc in range(NFC):
                            xbuf = xfc_bufs[fc % 2]
                            mbuf = xm_bufs[fc % 2]
                            t_x0 = P.add(Eng.SY,
                                         lambda sy, fc=fc, t0=t0, xbuf=xbuf:
                                         sy.dma_start(out=xbuf[:, :],
                                                      in_=x0[b, fc * 128:(fc + 1) * 128, t0:t0 + 512]),
                                         deps=[fins[fc - 2][0]] if fc >= 2 else prev_fins,
                                         tick=16)

                            def f_fin(v, fc=fc, t0=t0, xbuf=xbuf, mbuf=mbuf):
                                opc = op_a[:, fc * S + t0: fc * S + t0 + 512]
                                sc = sq_t[:, fc * 512:(fc + 1) * 512]
                                return [(v.tensor_mul(sc, opc, pba[:, :]), 1),
                                        (v.tensor_add(sc, sc, pbb[:, :]), 1),
                                        (v.tensor_add(mbuf[:, :], sc, xbuf[:, :]), 1)]
                            t_fin = P.add(Eng.VE, f_fin,
                                          deps=[t_bc, t_x0] +
                                          ([fins[fc - 2][1]] if fc >= 2 else []),
                                          tick=3)
                            t_xm = P.add(Eng.SY,
                                         lambda sy, fc=fc, t0=t0, mbuf=mbuf:
                                         sy.dma_start(out=xmid_d[fc * 128:(fc + 1) * 128, t0:t0 + 512],
                                                      in_=mbuf[:, :]),
                                         deps=[t_fin], tick=16)
                            fins.append((t_fin, t_xm))
                        prev_fins = [t for pair in fins for t in pair]
                    if DEBUG_DUMPS and b == 0:
                        P.add(Eng.SY, lambda sy: sy.dma_start(out=dbg["d_comb"][:, :], in_=comb_d[:, :]),
                              deps=[t_crel], tick=16)
                        P.add(Eng.SY, lambda sy: sy.dma_start(out=dbg["d_xmid"][:, :], in_=xmid_d[:, :]),
                              deps=prev_fins, tick=16)
                    P.emit(blk)

        with ExitStack() as fctx:
            ln2 = sbuf(f"ln2_{b}", [128, NFC * 1024], BF16, fctx)
            xmc = [sbuf(f"xmc{i}_{b}", [128, 1024], BF16, fctx) for i in range(2)]
            wgb = [sbuf(f"wgb0_{b}", [128, 2 * S], BF16, fctx),
                   sbuf(f"wgb1_{b}", [128, 2 * S], BF16, fctx)]
            harena = sbuf(f"harena_{b}", [128, GRP * 1024], BF16, fctx)
            wd_bufs = [sbuf(f"wdb{i}_{b}", [128, S], BF16, fctx) for i in range(GRP)]
            dsb = sbuf(f"dsb_{b}", [128, NFC * 1024], F32, fctx)
            sqc = [sbuf(f"sqc{i}_{b}", [128, 1024], BF16, fctx) for i in range(2)]
            stmp = sqc[0]
            octmp = [wd_bufs[0].bitcast(F32), wd_bufs[1].bitcast(F32)]

            for th in range(2):
                tc0 = th * 1024
                with ExitStack() as pc:
                    p_s = [pc.enter_context(nc.psum_tensor(f"f1s{i}_{b}_{th}", [128, 512], F32))
                           for i in range(4)]
                    pba = pc.enter_context(nc.psum_tensor(f"f1ba_{b}_{th}", [128, 1024], F32))
                    pbb = pc.enter_context(nc.psum_tensor(f"f1bb_{b}_{th}", [128, 1024], F32))
                    with nc.Block(f"c1_{b}_{th}", no_gpsimd_drain=True) as blk:
                        P = Prog(sems)
                        st_toks = []
                        xd_toks = []
                        for fc in range(NFC):
                            xb = xmc[fc % 2]
                            sq = sqc[fc % 2]
                            t_xd = P.add(Eng.SY,
                                         lambda sy, fc=fc, xb=xb: sy.dma_start(
                                             out=xb[:, :],
                                             in_=xmid_d[fc * 128:(fc + 1) * 128, tc0:tc0 + 1024]),
                                         deps=[st_toks[fc - 2] if fc >= 2 else None],
                                         tick=16)
                            xd_toks.append(t_xd)
                            t_q = P.add(Eng.VE,
                                        lambda v, xb=xb, sq=sq: v.tensor_mul(
                                            sq[:, :], xb[:, :], xb[:, :]),
                                        deps=[t_xd, st_toks[fc - 2] if fc >= 2 else None])

                            def f_st(t, fc=fc, xb=xb, sq=sq):
                                insts = []
                                for sl in range(2):
                                    insts.append((t.matmul(
                                        p_s[sl][0:1, :], ones_c[:, :],
                                        xb[:, sl * 512:(sl + 1) * 512],
                                        start=(fc == 0), stop=(fc == NFC - 1)), 0))
                                for sl in range(2):
                                    insts.append((t.matmul(
                                        p_s[2 + sl][0:1, :], ones_c[:, :],
                                        sq[:, sl * 512:(sl + 1) * 512],
                                        start=(fc == 0), stop=(fc == NFC - 1)),
                                        1 if sl == 1 else 0))
                                return insts
                            st_toks.append(P.add(Eng.PE, f_st, deps=[t_xd, t_q]))

                        rows = []
                        rd_prev = []
                        for sl in range(2):
                            t_rows, t_rd = norm_rows(P, p_s[sl], p_s[2 + sl], 512,
                                                     tc0 + sl * 512,
                                                     [st_toks[-1]] + rd_prev)
                            rows.append(t_rows)
                            rd_prev = [t_rd]

                        def f_bc(t):
                            insts = []
                            for sl in range(2):
                                insts.append((t.matmul(
                                    pba[:, sl * 512:(sl + 1) * 512], ones_r[:, :],
                                    rows_a[0:1, tc0 + sl * 512: tc0 + (sl + 1) * 512],
                                    start=True, stop=True), 1))
                                insts.append((t.matmul(
                                    pbb[:, sl * 512:(sl + 1) * 512], ones_r[:, :],
                                    rows_b2[0:1, tc0 + sl * 512: tc0 + (sl + 1) * 512],
                                    start=True, stop=True), 1))
                            return insts
                        t_bc = P.add(Eng.PE, f_bc, deps=rows, tick=4)

                        app_toks = []
                        for fc in range(NFC):
                            xb = xmc[fc % 2]
                            sq = sqc[fc % 2]
                            t_xd2 = P.add(Eng.SY,
                                          lambda sy, fc=fc, xb=xb: sy.dma_start(
                                              out=xb[:, :],
                                              in_=xmid_d[fc * 128:(fc + 1) * 128, tc0:tc0 + 1024]),
                                          deps=[st_toks[-1],
                                                app_toks[fc - 2] if fc >= 2 else None],
                                          tick=16)

                            def f_app(v, fc=fc, xb=xb, sq=sq):
                                return [(v.tensor_mul(sq[:, :], xb[:, :], pba[:, :]), 1),
                                        (v.tensor_add(ln2[:, fc * 1024:(fc + 1) * 1024],
                                                      sq[:, :], pbb[:, :]), 1)]
                            app_toks.append(P.add(Eng.VE, f_app, deps=[t_bc, t_xd2], tick=2))
                        P.emit(blk)

                with ExitStack() as pc:
                    p_g = pc.enter_context(nc.psum_tensor(f"f2g_{b}_{th}", [128, 1024], F32))
                    p_u = pc.enter_context(nc.psum_tensor(f"f2u_{b}_{th}", [128, 1024], F32))
                    p_d0 = pc.enter_context(nc.psum_tensor(f"f2d0_{b}_{th}", [128, 1024], F32))
                    p_d1 = pc.enter_context(nc.psum_tensor(f"f2d1_{b}_{th}", [128, 1024], F32))
                    p_d = [p_d0, p_d1]
                    with nc.Block(f"ffn_{b}_{th}", no_gpsimd_drain=True) as blk:
                        P = Prog(sems)
                        t_gu = {}
                        t_silu = {}
                        t_h = {}
                        t_wdc = {}
                        dmm_last = {}
                        dev = {}
                        for grp in range(NIC // GRP):
                            for ii in range(GRP):
                                ic = grp * GRP + ii
                                wbuf = wgb[ic % 2]
                                wg_ap = wbuf[:, 0:S]
                                wu_ap = wbuf[:, S:2 * S]
                                t_wl = P.add(Eng.GD,
                                             lambda g, ic=ic, wg_ap=wg_ap, wu_ap=wu_ap:
                                             [(g.dma_start(out=wg_ap, in_=wgt[ic, :, :]), 16),
                                              (g.dma_start(out=wu_ap, in_=wut[ic, :, :]), 16)],
                                             deps=[t_gu.get(ic - 2)], tick=32)
                                t_cv = P.add(Eng.VE,
                                             lambda v, wg_ap=wg_ap, wu_ap=wu_ap:
                                             [(v.tensor_scalar_add(wg_ap, wg_ap, -128.0), 1),
                                              (v.tensor_scalar_add(wu_ap, wu_ap, -128.0), 1)],
                                             deps=[t_wl], tick=2)

                                def f_gmm(t, wg_ap=wg_ap):
                                    for ns in range(2):
                                        for fc in range(NFC):
                                            mi = t.matmul(
                                                p_g[:, ns * 512:(ns + 1) * 512],
                                                wg_ap[:, fc * 128:(fc + 1) * 128],
                                                ln2[:, fc * 1024 + ns * 512: fc * 1024 + (ns + 1) * 512],
                                                start=(fc == 0), stop=(fc == NFC - 1))
                                    return mi

                                def f_umm(t, wu_ap=wu_ap):
                                    for ns in range(2):
                                        for fc in range(NFC):
                                            mi = t.matmul(
                                                p_u[:, ns * 512:(ns + 1) * 512],
                                                wu_ap[:, fc * 128:(fc + 1) * 128],
                                                ln2[:, fc * 1024 + ns * 512: fc * 1024 + (ns + 1) * 512],
                                                start=(fc == 0), stop=(fc == NFC - 1))
                                    return mi
                                t_g = P.add(Eng.PE, f_gmm, deps=[t_cv, t_silu.get(ic - 1)])
                                t_gu[ic] = P.add(Eng.PE, f_umm, deps=[t_h.get(ic - 1)])
                                t_silu[ic] = P.add(Eng.AC,
                                                   lambda s, ic=ic: s.activation(
                                                       stmp[:, :], p_g[:, :], AF.Silu,
                                                       scale=s_gsc[:, ic:ic + 1]),
                                                   deps=[t_g, t_h.get(ic - 1)])
                                h_ap = harena[:, ii * 1024:(ii + 1) * 1024]
                                t_h[ic] = P.add(Eng.VE,
                                                lambda v, ic=ic, h_ap=h_ap:
                                                v.scalar_tensor_tensor(
                                                    h_ap, p_u[:, :], s_usc[:, ic:ic + 1],
                                                    stmp[:, :], ALU.mult, ALU.mult),
                                                deps=[t_silu[ic], t_gu[ic], dmm_last.get(grp - 1)])
                                t_wd = P.add(Eng.GD,
                                             lambda g, ic=ic, ii=ii:
                                             g.dma_start(out=wd_bufs[ii][:, :], in_=wdt[ic, :, :]),
                                             deps=[dmm_last.get(grp - 1)], tick=16)
                                t_wdc[ii] = P.add(Eng.VE,
                                                  lambda v, ii=ii:
                                                  v.tensor_scalar_add(wd_bufs[ii][:, :],
                                                                      wd_bufs[ii][:, :], -128.0),
                                                  deps=[t_wd])

                            grp_deps = [t_h[grp * GRP + ii] for ii in range(GRP)] + \
                                       [t_wdc[ii] for ii in range(GRP)]
                            for oc in range(NFC):
                                def f_dmm(t, oc=oc):
                                    for ii in range(GRP):
                                        h_ap = harena[:, ii * 1024:(ii + 1) * 1024]
                                        for sl in range(2):
                                            mi = t.matmul(
                                                p_d[oc % 2][:, sl * 512:(sl + 1) * 512],
                                                wd_bufs[ii][:, oc * 128:(oc + 1) * 128],
                                                h_ap[:, sl * 512:(sl + 1) * 512],
                                                start=(ii == 0), stop=(ii == GRP - 1))
                                    return mi
                                pd_war = (dev.get((grp, oc - 2)) if oc >= 2
                                          else dev.get((grp - 1, 14 + oc)))
                                t_dmm = P.add(Eng.PE, f_dmm,
                                              deps=grp_deps + [pd_war])

                                def f_dev(v, oc=oc, grp=grp):
                                    insts = []
                                    for sl in range(2):
                                        dst = dsb[:, oc * 1024 + sl * 512: oc * 1024 + (sl + 1) * 512]
                                        src = p_d[oc % 2][:, sl * 512:(sl + 1) * 512]
                                        if grp == 0:
                                            insts.append((v.tensor_copy(dst, src), 1))
                                        else:
                                            insts.append((v.tensor_add(dst, dst, src), 1))
                                    return insts
                                dev[(grp, oc)] = P.add(Eng.VE, f_dev, deps=[t_dmm], tick=2)
                            dmm_last[grp] = t_dmm

                        last_devs = [dev[(NIC // GRP - 1, oc)] for oc in range(NFC)]

                        st_toks = []
                        cast_toks = {}
                        for oc in range(NFC):
                            dbf = sqc[oc % 2]
                            sqb = wgb[oc % 2][:, 0:1024]

                            def f_cast(v, oc=oc, dbf=dbf, sqb=sqb):
                                return [(v.tensor_single_scalar(
                                    dbf[:, :], dsb[:, oc * 1024:(oc + 1) * 1024],
                                    s_dsc[:, oc:oc + 1], ALU.mult), 1),
                                    (v.tensor_mul(sqb, dbf[:, :], dbf[:, :]), 1)]
                            t_cast = P.add(Eng.VE, f_cast,
                                           deps=[dev[(NIC // GRP - 1, oc)],
                                                 st_toks[oc - 2] if oc >= 2 else None,
                                                 t_gu.get(NIC - 1)],
                                           tick=2)
                            cast_toks[oc] = t_cast

                            def f_st(t, oc=oc, dbf=dbf, sqb=sqb):
                                insts = []
                                for sl in range(2):
                                    insts.append((t.matmul(
                                        p_g[0:1, sl * 512:(sl + 1) * 512], ones_c[:, :],
                                        dbf[:, sl * 512:(sl + 1) * 512],
                                        start=(oc == 0), stop=(oc == NFC - 1)), 0))
                                for sl in range(2):
                                    insts.append((t.matmul(
                                        p_u[0:1, sl * 512:(sl + 1) * 512], ones_c[:, :],
                                        sqb[:, sl * 512:(sl + 1) * 512],
                                        start=(oc == 0), stop=(oc == NFC - 1)),
                                        1 if sl == 1 else 0))
                                return insts
                            st_deps = [t_cast]
                            if oc == 0:
                                st_deps += [t_silu[NIC - 1], t_h[NIC - 1]]
                            st_toks.append(P.add(Eng.PE, f_st, deps=st_deps))

                        rows = []
                        rd_prev = []
                        for sl in range(2):
                            class PS:
                                pass
                            ps1 = p_g[0:1, sl * 512:(sl + 1) * 512]
                            ps2 = p_u[0:1, sl * 512:(sl + 1) * 512]

                            class RowAP:
                                def __init__(self, ap):
                                    self.ap = ap

                                def __getitem__(self, idx):
                                    return self.ap
                            t_rows, t_rd = norm_rows(P, RowAP(ps1), RowAP(ps2), 512,
                                                     tc0 + sl * 512,
                                                     [st_toks[-1]] + rd_prev)
                            rows.append(t_rows)
                            rd_prev = [t_rd]

                        def f_bc(t):
                            insts = []
                            for sl in range(2):
                                insts.append((t.matmul(
                                    p_d0[:, sl * 512:(sl + 1) * 512], ones_r[:, :],
                                    rows_a[0:1, tc0 + sl * 512: tc0 + (sl + 1) * 512],
                                    start=True, stop=True), 1))
                                insts.append((t.matmul(
                                    p_d1[:, sl * 512:(sl + 1) * 512], ones_r[:, :],
                                    rows_b2[0:1, tc0 + sl * 512: tc0 + (sl + 1) * 512],
                                    start=True, stop=True), 1))
                            return insts
                        t_bc = P.add(Eng.PE, f_bc, deps=rows + last_devs, tick=4)

                        fin_toks = {}
                        od_toks = {}
                        for fc in range(NFC):
                            ob = octmp[fc % 2]
                            xb = xmc[fc % 2]
                            t_xd3 = P.add(Eng.SY,
                                          lambda sy, fc=fc, xb=xb: sy.dma_start(
                                              out=xb[:, :],
                                              in_=xmid_d[fc * 128:(fc + 1) * 128, tc0:tc0 + 1024]),
                                          deps=[fin_toks.get(fc - 2)], tick=16)

                            def f_fin(v, fc=fc, ob=ob, xb=xb):
                                dchunk = dsb[:, fc * 1024:(fc + 1) * 1024]
                                return [(v.tensor_single_scalar(dchunk, dchunk,
                                                                s_dsc[:, fc:fc + 1], ALU.mult), 1),
                                        (v.tensor_mul(ob[:, :], dchunk, p_d0[:, :]), 1),
                                        (v.tensor_add(ob[:, :], ob[:, :], p_d1[:, :]), 1),
                                        (v.tensor_add(ob[:, :], ob[:, :], xb[:, :]), 1)]
                            t_fin = P.add(Eng.VE, f_fin,
                                          deps=[t_bc, t_xd3, od_toks.get(fc - 2)], tick=4)
                            fin_toks[fc] = t_fin
                            t_od = P.add(Eng.SY,
                                         lambda sy, fc=fc, ob=ob:
                                         sy.dma_start(out=out[b, fc * 128:(fc + 1) * 128, tc0:tc0 + 1024],
                                                      in_=ob[:, :]),
                                         deps=[t_fin], tick=16)
                            od_toks[fc] = t_od
                            out_dma_tokens.append(t_od)
                        P.emit(blk)

    with nc.Block("fin", no_gpsimd_drain=True) as blk:
        P = Prog(sems)
        P.add(Eng.GD, lambda g: g.engine_nop(), deps=out_dma_tokens)
        P.emit(blk)

    ctx.close()
    return nc


_CACHED = {}


def _host_stage(inputs):
    def npa(x, dt=None):
        a = np.asarray(x)
        return a.astype(dt) if dt is not None else a

    x = npa(inputs["L0_attn_hidden_states"], np.float32)
    xT = np.ascontiguousarray(x.transpose(0, 2, 1))

    def pack_pairs(W):
        WT = npa(W, np.float32).T
        Wr = WT.reshape(NFC, 128, H)
        blocks = []
        for p in range(8):
            blk = np.concatenate([Wr[:, :, 128 * p:128 * (p + 1)],
                                  Wr[:, :, 128 * (p + 8):128 * (p + 9)]], axis=2)
            blocks.append(np.ascontiguousarray(blk.transpose(1, 0, 2)).reshape(128, NFC * 256))
        return np.ascontiguousarray(np.stack(blocks))

    wq_h = pack_pairs(inputs["L0_attn_Wq"])
    wk_h = pack_pairs(inputs["L0_attn_Wk"])
    wv_h = pack_pairs(inputs["L0_attn_Wv"])

    WoT = npa(inputs["L0_attn_Wo"], np.float32).T
    Wr = WoT.reshape(NFC, 128, H)
    wo_h = np.ascontiguousarray(np.stack(
        [np.ascontiguousarray(Wr[:, :, 256 * ob:256 * (ob + 1)].transpose(1, 0, 2))
         .reshape(128, NFC * 256) for ob in range(8)]))

    j = np.arange(0, H, 2, dtype=np.float64) / H
    inv_freq = 1.0 / (10000.0 ** j)
    ang = np.outer(np.arange(S, dtype=np.float64), inv_freq)
    cosT = np.ascontiguousarray(np.cos(ang).T).astype(ml_dtypes.bfloat16)
    sinT = np.ascontiguousarray(np.sin(ang).T).astype(ml_dtypes.bfloat16)

    def pack_ffn(Wq_):
        A = npa(Wq_, np.int32).reshape(NIC, 128, NFC, 128)
        return np.ascontiguousarray(A.transpose(0, 3, 2, 1)).reshape(NIC, 128, NFC * 128)

    wg_h = pack_ffn(inputs["L0_ffn_gate_w"])
    wu_h = pack_ffn(inputs["L0_ffn_up_w"])
    wd_h = np.ascontiguousarray(npa(inputs["L0_ffn_down_w"], np.int32).T).reshape(NIC, 128, H)

    gsc_h = np.ascontiguousarray(npa(inputs["L0_ffn_gate_scale"], np.float32).reshape(NIC, 128).T)
    usc_h = np.ascontiguousarray(npa(inputs["L0_ffn_up_scale"], np.float32).reshape(NIC, 128).T)
    dsc_h = np.ascontiguousarray(npa(inputs["L0_ffn_down_scale"], np.float32).reshape(NFC, 128).T)
    gate_h = np.ascontiguousarray(npa(inputs["L0_attn_gate"], np.float32).reshape(1, NH))
    ident = np.eye(128, dtype=np.float32).astype(ml_dtypes.bfloat16)

    shared = dict(wq=wq_h, wk=wk_h, wv=wv_h, wo=wo_h, cosT=cosT, sinT=sinT,
                  wgt=wg_h, wut=wu_h, wdt=wd_h, gsc=gsc_h, usc=usc_h, dsc=dsc_h,
                  gate=gate_h, ident=ident)
    in_maps = []
    for c in range(8):
        m = dict(shared)
        m["x0"] = np.ascontiguousarray(xT[B_LOC * c:B_LOC * (c + 1)])
        in_maps.append(m)
    return in_maps


def kernel(**inputs):
    if "nc" not in _CACHED:
        _CACHED["nc"] = build_program()
    nc = _CACHED["nc"]
    in_maps = _host_stage(inputs)
    res = run_bass_kernel_spmd(nc, in_maps, core_ids=list(range(8)))
    outT = np.concatenate([np.asarray(r["out"]) for r in res.results], axis=0)
    return np.ascontiguousarray(outT.transpose(0, 2, 1)).astype(np.float32)
